# version 01 (final)
# baseline (speedup 1.0000x reference)
import contextlib
import numpy as np
import concourse.bass as bass
import concourse.mybir as mybir
from concourse.bass_utils import run_bass_kernel_spmd

F32 = mybir.dt.float32
BF16 = mybir.dt.bfloat16
AF = mybir.ActivationFunctionType
OP = mybir.AluOpType
AX = mybir.AxisListType

NCORES = 8
B, L, D = 4, 4096, 4096
TOK = B * L
TPC = TOK // NCORES
DI = 8192
H, PD, G, N = 128, 64, 8, 128
CONVD = DI + 2 * G * N
DIN = 2 * DI + 2 * G * N + H
CH = 128
NCH = L // CH
DS1 = 16
DTR = 128
EPS = 1e-5



class Prog:

    def __init__(self):
        self.ops = []

    def add(self, eng, fn, inc=1, wait=True):
        self.ops.append((eng, fn, inc, wait))

    def dma(self, out, in_, q="sync", **kw):
        self.add(q, lambda e, o=out, i=in_, k=kw: e.dma_start(o, i, **k), inc=16)

    def mm(self, out, lhsT, rhs, first=True, last=True):
        self.add("tensor",
                 lambda e, o=out, l=lhsT, r=rhs, s=first, t=last:
                 e.matmul(o, l, r, start=s, stop=t),
                 inc=(1 if last else 0), wait=first)

    def trans(self, out, in_, ident):
        self.add("tensor", lambda e, o=out, i=in_, d=ident: e.transpose(o, i, d))

    def tt(self, out, a, b, op, eng="vector"):
        self.add(eng, lambda e, o=out, x=a, y=b, p=op: e.tensor_tensor(o, x, y, p))

    def ts(self, out, in0, s1, s2, op0, op1=None, eng="vector"):
        self.add(eng, lambda e, o=out, i=in0, a=s1, b=s2, p0=op0, p1=op1:
                 (e.tensor_scalar(o, i, a, b, p0, p1) if p1 is not None
                  else e.tensor_single_scalar(o, i, a, p0)))

    def stt(self, out, in0, scalar, in1, op0, op1, eng="vector"):
        self.add(eng, lambda e, o=out, i=in0, s=scalar, j=in1, p0=op0, p1=op1:
                 e.scalar_tensor_tensor(o, i, s, j, p0, p1))

    def act(self, out, in_, func, bias=None, scale=None, accum_out=None):
        def f(e, o=out, i=in_, fn=func, b=bias, s=scale, ac=accum_out):
            kw = {}
            if b is not None:
                kw["bias"] = b
            if s is not None:
                kw["scale"] = s
            if ac is not None:
                kw["accum_out"] = ac
            return e.activation(o, i, fn, **kw)
        self.add("scalar", f)

    def copy(self, out, in_, eng="scalar"):
        self.add(eng, lambda e, o=out, i=in_: e.copy(o, i) if eng == "scalar"
                 else e.tensor_copy(o, i))

    def vcopy(self, out, in_):
        self.add("vector", lambda e, o=out, i=in_: e.tensor_copy(o, i))

    def memset(self, ap, v):
        self.add("vector", lambda e, a=ap, c=v: e.memset(a, c))

    def red(self, out, in_, op):
        self.add("vector", lambda e, o=out, i=in_, p=op: e.tensor_reduce(o, i, AX.X, p))

    def emit(self, nc):
        pre = []
        c = 0
        for (_, _, inc, _) in self.ops:
            pre.append(c)
            c += inc
        total = c
        ops = self.ops
        with nc.semaphore("order") as sem, nc.Block() as block:
            def mk(name):
                def section(e):
                    for (op, p) in zip(ops, pre):
                        eng, fn, inc, wait = op
                        if eng != name:
                            continue
                        if wait and p > 0:
                            e.wait_ge(sem, p)
                        inst = fn(e)
                        if inc:
                            inst.then_inc(sem, inc)
                    e.wait_ge(sem, total)
                return section
            block.sync(mk("sync"))
            block.tensor(mk("tensor"))
            block.vector(mk("vector"))
            block.scalar(mk("scalar"))
            block.gpsimd(mk("gpsimd"))
        return nc


def runl(build, in_maps):
    nc = bass.Bass("TRN2", target_bir_lowering=False)
    build(nc)
    res = run_bass_kernel_spmd(nc, in_maps, core_ids=list(range(NCORES)))
    return res.results


def ct(a):
    return np.ascontiguousarray(a, dtype=np.float32)



def build_l1(nc):
    x_in = nc.dram_tensor("x", [TPC, D], F32, kind="ExternalInput").ap()
    winT = nc.dram_tensor("winT", [D, DIN], F32, kind="ExternalInput").ap()
    wupT = nc.dram_tensor("wupT", [D, D], F32, kind="ExternalInput").ap()
    gateT = nc.dram_tensor("gateT", [D, 4], F32, kind="ExternalInput").ap()
    ebias = nc.dram_tensor("ebias", [128, 4], F32, kind="ExternalInput").ap()
    laT = nc.dram_tensor("laT", [D, 32], F32, kind="ExternalInput").ap()
    lbT = nc.dram_tensor("lbT", [8, 4 * D], F32, kind="ExternalInput").ap()
    eye = nc.dram_tensor("eye", [128, 128], F32, kind="ExternalInput").ap()
    xn_o = nc.dram_tensor("xn_o", [TPC, D], F32, kind="ExternalOutput").ap()
    zT_o = nc.dram_tensor("zT_o", [DIN, TPC], F32, kind="ExternalOutput").ap()
    ffn_o = nc.dram_tensor("ffn_o", [TPC, D], F32, kind="ExternalOutput").ap()

    p = Prog()
    with contextlib.ExitStack() as _st:
        s_x = _st.enter_context(nc.sbuf_tensor("s_x", [128, D], F32))
        s_xc = _st.enter_context(nc.sbuf_tensor("s_xc", [128, D], F32))
        s_xn = _st.enter_context(nc.sbuf_tensor("s_xn", [128, D], F32))
        s_xnbf = _st.enter_context(nc.sbuf_tensor("s_xnbf", [128, D], BF16))
        s_m = _st.enter_context(nc.sbuf_tensor("s_m", [128, 8], F32))
        s_xnT = _st.enter_context(nc.sbuf_tensor("s_xnT", [128, 32 * 1024], BF16))
        s_wstage = _st.enter_context(nc.sbuf_tensor("s_wstage", [128, 512], F32))
        s_wbf = _st.enter_context(nc.sbuf_tensor("s_wbf", [128, 32 * 512], BF16))
        s_out = _st.enter_context(nc.sbuf_tensor("s_out", [128, 512], F32))
        s_eyef = _st.enter_context(nc.sbuf_tensor("s_eyef", [128, 128], F32))
        s_eyebf = _st.enter_context(nc.sbuf_tensor("s_eyebf", [128, 128], BF16))
        s_gatebf = _st.enter_context(nc.sbuf_tensor("s_gatebf", [128, 128], BF16))
        s_laTbf = _st.enter_context(nc.sbuf_tensor("s_laTbf", [128, 32 * 32], BF16))
        s_lbbf = _st.enter_context(nc.sbuf_tensor("s_lbbf", [8, 4 * D], BF16))
        s_ebias = _st.enter_context(nc.sbuf_tensor("s_ebias", [128, 4], F32))
        s_gsb = _st.enter_context(nc.sbuf_tensor("s_gsb", [128, 4], F32))
        s_oh = _st.enter_context(nc.sbuf_tensor("s_oh", [128, 4], F32))
        s_z = _st.enter_context(nc.sbuf_tensor("s_z", [128, 32], F32))
        s_zmbf = _st.enter_context(nc.sbuf_tensor("s_zmbf", [128, 32], BF16))
        s_zmT = _st.enter_context(nc.sbuf_tensor("s_zmT", [8, 8 * 4 * 128], BF16))
        p_mm = _st.enter_context(nc.psum_tensor("p_mm", [128, 512], F32))
        p_mm2 = _st.enter_context(nc.psum_tensor("p_mm2", [128, 512], F32))
        p_t = _st.enter_context(nc.psum_tensor("p_t", [128, 128], BF16))
        p_gate = _st.enter_context(nc.psum_tensor("p_gate", [128, 4], F32))
        p_z = _st.enter_context(nc.psum_tensor("p_z", [128, 32], F32))
        p_zT = _st.enter_context(nc.psum_tensor("p_zT", [8, 128], BF16))
        p.memset(s_m[:, 4:5], EPS)
        p.dma(s_eyef[:, :], eye[:, :])
        p.vcopy(s_eyebf[:, :], s_eyef[:, :])
        p.dma(s_ebias[:, :], ebias[:, :])
        for k in range(32):
            p.dma(s_wstage[:, 0:4], gateT[k * 128:(k + 1) * 128, :])
            p.vcopy(s_gatebf[:, k * 4:k * 4 + 4], s_wstage[:, 0:4])
            p.dma(s_wstage[:, 0:32], laT[k * 128:(k + 1) * 128, :])
            p.vcopy(s_laTbf[:, k * 32:k * 32 + 32], s_wstage[:, 0:32])
        for e in range(4):
            p.dma(s_x[0:8, 0:D], lbT[:, e * D:(e + 1) * D])
            p.vcopy(s_lbbf[:, e * D:(e + 1) * D], s_x[0:8, 0:D])

        for tb in range(2):
            for j in range(8):
                t = tb * 8 + j
                p.dma(s_x[:, :], x_in[t * 128:(t + 1) * 128, :])
                p.red(s_m[:, 0:1], s_x[:, :], OP.add)
                p.act(s_m[:, 1:2], s_m[:, 0:1], AF.Copy, scale=1.0 / D)
                p.ts(s_xc[:, :], s_x[:, :], s_m[:, 1:2], None, OP.subtract)
                p.act(s_xn[:, :], s_xc[:, :], AF.Square)
                p.red(s_m[:, 2:3], s_xn[:, :], OP.add)
                p.act(s_m[:, 3:4], s_m[:, 2:3], AF.Ln, scale=1.0 / D,
                      bias=s_m[:, 4:5])
                p.act(s_m[:, 0:1], s_m[:, 3:4], AF.Exp, scale=-0.5)
                p.ts(s_xn[:, :], s_xc[:, :], s_m[:, 0:1], None, OP.mult)
                p.dma(xn_o[t * 128:(t + 1) * 128, :], s_xn[:, :])
                p.vcopy(s_xnbf[:, :], s_xn[:, :])
                for k in range(32):
                    p.trans(p_t[:, :], s_xnbf[:, k * 128:(k + 1) * 128], s_eyebf[:, :])
                    p.copy(s_xnT[:, k * 1024 + j * 128:k * 1024 + (j + 1) * 128],
                           p_t[:, :])
            for mb in range(37):
                cols = 512 if mb < 36 else 128
                for k in range(32):
                    p.dma(s_wstage[:, 0:cols],
                          winT[k * 128:(k + 1) * 128, mb * 512:mb * 512 + cols])
                    p.vcopy(s_wbf[:, k * 512:k * 512 + cols], s_wstage[:, 0:cols])
                for m in range(cols // 128):
                    for n in range(2):
                        pm = p_mm if n == 0 else p_mm2
                        for k in range(32):
                            p.mm(pm[:, 0:512],
                                 s_wbf[:, k * 512 + m * 128:k * 512 + (m + 1) * 128],
                                 s_xnT[:, k * 1024 + n * 512:k * 1024 + (n + 1) * 512],
                                 first=(k == 0), last=(k == 31))
                        p.copy(s_out[:, :], pm[:, 0:512])
                        p.dma(zT_o[mb * 512 + m * 128:mb * 512 + (m + 1) * 128,
                                   tb * 1024 + n * 512:tb * 1024 + (n + 1) * 512],
                              s_out[:, :])
            for j in range(8):
                for k in range(32):
                    p.mm(p_gate[:, :],
                         s_xnT[:, k * 1024 + j * 128:k * 1024 + (j + 1) * 128],
                         s_gatebf[:, k * 4:(k + 1) * 4],
                         first=(k == 0), last=(k == 31))
                p.tt(s_gsb[:, :], p_gate[:, :], s_ebias[:, :], OP.add)
                p.red(s_m[:, 0:1], s_gsb[:, :], OP.max)
                p.ts(s_oh[:, 0:1], s_gsb[:, 0:1], s_m[:, 0:1], None, OP.is_ge)
                p.vcopy(s_m[:, 1:2], s_oh[:, 0:1])
                for e in range(1, 4):
                    p.ts(s_oh[:, e:e + 1], s_gsb[:, e:e + 1], s_m[:, 0:1], None,
                         OP.is_ge)
                    p.ts(s_m[:, 2:3], s_m[:, 1:2], -1.0, 1.0, OP.mult, OP.add)
                    p.tt(s_oh[:, e:e + 1], s_oh[:, e:e + 1], s_m[:, 2:3], OP.mult)
                    p.tt(s_m[:, 1:2], s_m[:, 1:2], s_oh[:, e:e + 1], OP.add)
                for k in range(32):
                    p.mm(p_z[:, :],
                         s_xnT[:, k * 1024 + j * 128:k * 1024 + (j + 1) * 128],
                         s_laTbf[:, k * 32:(k + 1) * 32],
                         first=(k == 0), last=(k == 31))
                p.copy(s_z[:, :], p_z[:, :])
                for e in range(4):
                    p.ts(s_zmbf[:, e * 8:(e + 1) * 8], s_z[:, e * 8:(e + 1) * 8],
                         s_oh[:, e:e + 1], None, OP.mult)
                for e in range(4):
                    p.trans(p_zT[0:8, :], s_zmbf[:, e * 8:(e + 1) * 8], s_eyebf[:, :])
                    p.copy(s_zmT[:, (j * 4 + e) * 128:(j * 4 + e + 1) * 128],
                           p_zT[0:8, :])
            for ns in range(8):
                for k in range(32):
                    p.dma(s_wstage[:, :],
                          wupT[k * 128:(k + 1) * 128, ns * 512:(ns + 1) * 512])
                    p.vcopy(s_wbf[:, k * 512:(k + 1) * 512], s_wstage[:, :])
                for j in range(8):
                    for k in range(32):
                        p.mm(p_mm[:, :],
                             s_xnT[:, k * 1024 + j * 128:k * 1024 + (j + 1) * 128],
                             s_wbf[:, k * 512:(k + 1) * 512],
                             first=(k == 0), last=False)
                    for e in range(4):
                        p.mm(p_mm[:, :],
                             s_zmT[:, (j * 4 + e) * 128:(j * 4 + e + 1) * 128],
                             s_lbbf[:, e * D + ns * 512:e * D + (ns + 1) * 512],
                             first=False, last=(e == 3))
                    p.copy(s_out[:, :], p_mm[:, :])
                    p.dma(ffn_o[(tb * 8 + j) * 128:(tb * 8 + j + 1) * 128,
                                ns * 512:(ns + 1) * 512], s_out[:, :])
        p.emit(nc)
    return nc



def build_l2(nc):
    xp = nc.dram_tensor("xp", [5120, L + 3], F32, kind="ExternalInput").ap()
    cw = nc.dram_tensor("cw", [5120, 4], F32, kind="ExternalInput").ap()
    cb = nc.dram_tensor("cb", [5120, 1], F32, kind="ExternalInput").ap()
    xo = nc.dram_tensor("xo", [5120, L], F32, kind="ExternalOutput").ap()
    p = Prog()
    with contextlib.ExitStack() as _st:
        s_in = _st.enter_context(nc.sbuf_tensor("c_in", [128, L + 3], F32))
        s_w = _st.enter_context(nc.sbuf_tensor("c_w", [128, 4], F32))
        s_b = _st.enter_context(nc.sbuf_tensor("c_b", [128, 1], F32))
        s_acc = _st.enter_context(nc.sbuf_tensor("c_acc", [128, L], F32))
        s_o = _st.enter_context(nc.sbuf_tensor("c_o", [128, L], F32))
        for t in range(40):
            r0 = t * 128
            p.dma(s_in[:, :], xp[r0:r0 + 128, :])
            p.dma(s_w[:, :], cw[r0:r0 + 128, :])
            p.dma(s_b[:, :], cb[r0:r0 + 128, :])
            p.ts(s_acc[:, :], s_in[:, 0:L], s_w[:, 0:1], None, OP.mult)
            for k in range(1, 4):
                p.stt(s_acc[:, :], s_in[:, k:k + L], s_w[:, k:k + 1], s_acc[:, :],
                      OP.mult, OP.add)
            p.act(s_o[:, :], s_acc[:, :], AF.Silu, bias=s_b[:, 0:1])
            p.dma(xo[r0:r0 + 128, :], s_o[:, :])
        p.emit(nc)
    return nc



def build_l3(nc):
    x_tm = nc.dram_tensor("x_tm", [L, 4096], F32, kind="ExternalInput").ap()
    b_cm = nc.dram_tensor("b_cm", [512, L], F32, kind="ExternalInput").ap()
    c_cm = nc.dram_tensor("c_cm", [512, L], F32, kind="ExternalInput").ap()
    bt = nc.dram_tensor("bt", [L, 512], F32, kind="ExternalInput").ap()
    dtr = nc.dram_tensor("dtr", [L, 64], F32, kind="ExternalInput").ap()
    arep = nc.dram_tensor("arep", [128, 64], F32, kind="ExternalInput").ap()
    dtb = nc.dram_tensor("dtb", [128, 64], F32, kind="ExternalInput").ap()
    drep = nc.dram_tensor("drep", [128, 64], F32, kind="ExternalInput").ap()
    tri = nc.dram_tensor("tri", [128, 128], F32, kind="ExternalInput").ap()
    mka = nc.dram_tensor("mka", [128, 128], F32, kind="ExternalInput").ap()
    on1 = nc.dram_tensor("on1", [1, 128], F32, kind="ExternalInput").ap()
    y_o = nc.dram_tensor("y_o", [L, 4096], F32, kind="ExternalOutput").ap()
    scr = nc.dram_tensor("scr", [1, 8192], F32, kind="Internal").ap()

    p = Prog()
    with contextlib.ExitStack() as _st:
        s_tri = _st.enter_context(nc.sbuf_tensor("t_tri", [128, 128], F32))
        s_mask = _st.enter_context(nc.sbuf_tensor("t_mask", [128, 128], F32))
        s_ones = _st.enter_context(nc.sbuf_tensor("t_ones", [1, 128], F32))
        s_A = _st.enter_context(nc.sbuf_tensor("t_A", [128, 64], F32))
        s_dtb = _st.enter_context(nc.sbuf_tensor("t_dtb", [128, 64], F32))
        s_D = _st.enter_context(nc.sbuf_tensor("t_D", [128, 64], F32))
        sp_res = _st.enter_context(nc.sbuf_tensor("t_sp", [128, 32 * 64], F32))
        dA_res = _st.enter_context(nc.sbuf_tensor("t_dA", [128, 32 * 64], F32))
        s_S = _st.enter_context(nc.sbuf_tensor("t_S", [128, 64 * 64], F32))
        s_w1 = _st.enter_context(nc.sbuf_tensor("t_w1", [128, 64], F32))
        s_w2 = _st.enter_context(nc.sbuf_tensor("t_w2", [128, 64], F32))
        s_w3 = _st.enter_context(nc.sbuf_tensor("t_w3", [128, 64], F32))
        s_w4 = _st.enter_context(nc.sbuf_tensor("t_w4", [128, 64], F32))
        s_csi = _st.enter_context(nc.sbuf_tensor("t_csi", [128, 64], F32))
        s_odi = _st.enter_context(nc.sbuf_tensor("t_odi", [128, 64], F32))
        s_cshm = _st.enter_context(nc.sbuf_tensor("t_cshm", [64, 128], F32))
        s_csrow = _st.enter_context(nc.sbuf_tensor("t_csrow", [1, 8192], F32))
        s_cdrow = _st.enter_context(nc.sbuf_tensor("t_cdrow", [1, 64], F32))
        s_cd = _st.enter_context(nc.sbuf_tensor("t_cd", [128, 64], F32))
        s_bt = _st.enter_context(nc.sbuf_tensor("t_bt", [128, 512], F32))
        s_bg = _st.enter_context(nc.sbuf_tensor("t_bg", [128, 128], F32))
        s_bgbf = _st.enter_context(nc.sbuf_tensor("t_bgbf", [128, 128], BF16))
        s_cg = _st.enter_context(nc.sbuf_tensor("t_cg", [128, 128], F32))
        s_cgbf = _st.enter_context(nc.sbuf_tensor("t_cgbf", [128, 128], BF16))
        s_xg = _st.enter_context(nc.sbuf_tensor("t_xg", [128, 1024], F32))
        s_yg = _st.enter_context(nc.sbuf_tensor("t_yg", [128, 1024], F32))
        s_dt1 = _st.enter_context(nc.sbuf_tensor("t_dt1", [128, 128], F32))
        s_dt2 = _st.enter_context(nc.sbuf_tensor("t_dt2", [128, 128], F32))
        s_R = _st.enter_context(nc.sbuf_tensor("t_R", [128, 128], F32))
        s_mt = _st.enter_context(nc.sbuf_tensor("t_mt", [128, 128], BF16))
        s_xdt = _st.enter_context(nc.sbuf_tensor("t_xdt", [128, 64], F32))
        s_xdtbf = _st.enter_context(nc.sbuf_tensor("t_xdtbf", [128, 64], BF16))
        s_sbf = _st.enter_context(nc.sbuf_tensor("t_sbf", [128, 64], BF16))
        s_bds = _st.enter_context(nc.sbuf_tensor("t_bds", [128, 128], BF16))
        s_y1 = _st.enter_context(nc.sbuf_tensor("t_y1", [128, 64], F32))
        s_y2 = _st.enter_context(nc.sbuf_tensor("t_y2", [128, 64], F32))
        p_cs = _st.enter_context(nc.psum_tensor("q_cs", [128, 64], F32))
        p_cshm = _st.enter_context(nc.psum_tensor("q_cshm", [64, 128], F32))
        p_cd = _st.enter_context(nc.psum_tensor("q_cd", [128, 64], F32))
        p_g = _st.enter_context(nc.psum_tensor("q_g", [128, 128], F32))
        p_rb = _st.enter_context(nc.psum_tensor("q_rb", [128, 512], F32))
        p_y = _st.enter_context(nc.psum_tensor("q_y", [128, 64], F32))
        p_y2 = _st.enter_context(nc.psum_tensor("q_y2", [128, 64], F32))
        p_s = _st.enter_context(nc.psum_tensor("q_s", [128, 64], F32))
        p.dma(s_tri[:, :], tri[:, :])
        p.dma(s_mask[:, :], mka[:, :])
        p.dma(s_ones[:, :], on1[:, :])
        p.dma(s_dtb[:, :], dtb[:, :])
        p.dma(s_D[:, :], drep[:, :])
        p.dma(s_w1[:, :], arep[:, :])
        p.act(s_w2[:, :], s_w1[:, :], AF.Exp)
        p.ts(s_A[:, :], s_w2[:, :], -1.0, None, OP.mult)
        p.memset(s_S[:, :], 0.0)
        for i in range(32):
            sl = slice(i * 64, (i + 1) * 64)
            p.dma(s_w1[:, :], dtr[i * 128:(i + 1) * 128, :])
            p.tt(s_w2[:, :], s_w1[:, :], s_dtb[:, :], OP.add)
            p.act(s_w3[:, :], s_w2[:, :], AF.Abs)
            p.act(s_w4[:, :], s_w3[:, :], AF.Exp, scale=-1.0)
            p.act(s_w3[:, :], s_w4[:, :], AF.Ln, bias=1.0)
            p.add("vector", lambda e, o=s_w4, i_=s_w2: e.tensor_relu(o[:, :], i_[:, :]))
            p.tt(sp_res[:, sl], s_w4[:, :], s_w3[:, :], OP.add)
            p.tt(dA_res[:, sl], sp_res[:, sl], s_A[:, :], OP.mult)
        for i in range(32):
            sl = slice(i * 64, (i + 1) * 64)
            tc = slice(i * 128, (i + 1) * 128)
            p.mm(p_cs[:, :], s_tri[:, :], dA_res[:, sl])
            p.copy(s_csi[:, :], p_cs[:, :])
            p.act(s_odi[:, :], s_csi[:, :], AF.Exp)
            p.mm(p_cshm[:, :], dA_res[:, sl], s_tri[:, :])
            p.copy(s_cshm[:, :], p_cshm[:, :])
            p.dma(scr.tensor.ap().rearrange("o (a b) -> (o a) b", a=64, b=128),
                  s_cshm[:, :])
            p.dma(s_csrow[:, :], scr)
            p.dma(s_cdrow[:, :], s_odi[127:128, :])
            p.mm(p_cd[:, :], s_ones[:, :], s_cdrow[0:1, 0:64])
            p.copy(s_cd[:, :], p_cd[:, :])
            p.dma(s_bt[:, :], bt[tc, :])
            for g in range(4):
                p.dma(s_bg[:, :], b_cm[g * 128:(g + 1) * 128, tc])
                p.vcopy(s_bgbf[:, :], s_bg[:, :])
                p.dma(s_cg[:, :], c_cm[g * 128:(g + 1) * 128, tc])
                p.vcopy(s_cgbf[:, :], s_cg[:, :])
                p.dma(s_xg[:, :], x_tm[tc, g * 1024:(g + 1) * 1024])
                p.mm(p_g[:, :], s_bgbf[:, :], s_cgbf[:, :])
                for q in range(4):
                    p.mm(p_rb[:, :], s_ones[:, :],
                         s_csrow[0:1, (g * 16 + q * 4) * 128:(g * 16 + q * 4 + 4) * 128])
                    for w in range(4):
                        h = g * 16 + q * 4 + w
                        hh = q * 4 + w
                        p.ts(s_dt1[:, :], p_rb[:, w * 128:(w + 1) * 128],
                             s_csi[:, h:h + 1], 0.0, OP.subtract, OP.min)
                        p.tt(s_dt2[:, :], s_dt1[:, :], s_mask[:, :], OP.add)
                        p.act(s_R[:, :], s_dt2[:, :], AF.Exp)
                        p.tt(s_mt[:, :], p_g[:, :], s_R[:, :], OP.mult)
                        p.ts(s_xdt[:, :], s_xg[:, hh * 64:(hh + 1) * 64],
                             sp_res[:, i * 64 + h:i * 64 + h + 1], None, OP.mult)
                        p.vcopy(s_xdtbf[:, :], s_xdt[:, :])
                        p.mm(p_y[:, :], s_mt[:, :], s_xdtbf[:, :])
                        p.vcopy(s_sbf[:, :], s_S[:, h * 64:(h + 1) * 64])
                        p.mm(p_y2[:, :], s_cgbf[:, :], s_sbf[:, :])
                        p.ts(s_bds[:, :], s_bt[:, g * 128:(g + 1) * 128],
                             s_R[:, 127:128], None, OP.mult)
                        p.mm(p_s[:, :], s_bds[:, :], s_xdtbf[:, :])
                        p.ts(s_y1[:, :], p_y2[:, :], s_odi[:, h:h + 1], None, OP.mult)
                        p.tt(s_y2[:, :], s_y1[:, :], p_y[:, :], OP.add)
                        p.stt(s_yg[:, hh * 64:(hh + 1) * 64],
                              s_xg[:, hh * 64:(hh + 1) * 64], s_D[:, h:h + 1],
                              s_y2[:, :], OP.mult, OP.add)
                        p.stt(s_S[:, h * 64:(h + 1) * 64], s_S[:, h * 64:(h + 1) * 64],
                              s_cd[:, h:h + 1], p_s[:, :], OP.mult, OP.add)
                p.dma(y_o[tc, g * 1024:(g + 1) * 1024], s_yg[:, :])
        p.emit(nc)
    return nc



def build_l4a(nc):
    Y = nc.dram_tensor("Y", [TPC, DI], F32, kind="ExternalInput").ap()
    z = nc.dram_tensor("z", [TPC, DI], F32, kind="ExternalInput").ap()
    nw = nc.dram_tensor("nw", [128, DI], F32, kind="ExternalInput").ap()
    yn_o = nc.dram_tensor("yn_o", [TPC, DI], F32, kind="ExternalOutput").ap()
    p = Prog()
    with contextlib.ExitStack() as _st:
        s_Y = _st.enter_context(nc.sbuf_tensor("a_Y", [128, DI], F32))
        s_z = _st.enter_context(nc.sbuf_tensor("a_z", [128, DI], F32))
        s_t1 = _st.enter_context(nc.sbuf_tensor("a_t1", [128, DI], F32))
        s_y = _st.enter_context(nc.sbuf_tensor("a_y", [128, DI], F32))
        s_nw = _st.enter_context(nc.sbuf_tensor("a_nw", [128, DI], F32))
        s_m = _st.enter_context(nc.sbuf_tensor("a_m", [128, 8], F32))
        p.memset(s_m[:, 4:5], EPS)
        p.dma(s_nw[:, :], nw[:, :])
        for t in range(16):
            r = slice(t * 128, (t + 1) * 128)
            p.dma(s_Y[:, :], Y[r, :])
            p.dma(s_z[:, :], z[r, :])
            p.act(s_t1[:, :], s_z[:, :], AF.Silu)
            p.tt(s_y[:, :], s_Y[:, :], s_t1[:, :], OP.mult)
            p.act(s_t1[:, :], s_y[:, :], AF.Square)
            p.red(s_m[:, 0:1], s_t1[:, :], OP.add)
            p.act(s_m[:, 1:2], s_m[:, 0:1], AF.Ln, scale=1.0 / DI,
                  bias=s_m[:, 4:5])
            p.act(s_m[:, 2:3], s_m[:, 1:2], AF.Exp, scale=-0.5)
            p.ts(s_t1[:, :], s_y[:, :], s_m[:, 2:3], None, OP.mult)
            p.tt(s_y[:, :], s_t1[:, :], s_nw[:, :], OP.mult)
            p.dma(yn_o[r, :], s_y[:, :])
        p.emit(nc)
    return nc



def build_l4b(nc):
    ynT = nc.dram_tensor("ynT", [DI, TPC], F32, kind="ExternalInput").ap()
    opT = nc.dram_tensor("opT", [DI, D], F32, kind="ExternalInput").ap()
    xn = nc.dram_tensor("xn", [TPC, D], F32, kind="ExternalInput").ap()
    x0 = nc.dram_tensor("x0", [TPC, D], F32, kind="ExternalInput").ap()
    f0 = nc.dram_tensor("f0", [TPC, D], F32, kind="ExternalInput").ap()
    x1_o = nc.dram_tensor("x1_o", [TPC, D], F32, kind="ExternalOutput").ap()
    p = Prog()
    with contextlib.ExitStack() as _st:
        s_yst = _st.enter_context(nc.sbuf_tensor("b_yst", [128, 512], F32))
        s_ybf = _st.enter_context(nc.sbuf_tensor("b_ybf", [128, 64 * 512], BF16))
        s_wst = _st.enter_context(nc.sbuf_tensor("b_wst", [128, 512], F32))
        s_wbf = _st.enter_context(nc.sbuf_tensor("b_wbf", [128, 64 * 512], BF16))
        s_xn = _st.enter_context(nc.sbuf_tensor("b_xn", [128, 512], F32))
        s_x = _st.enter_context(nc.sbuf_tensor("b_x", [128, 512], F32))
        s_f = _st.enter_context(nc.sbuf_tensor("b_f", [128, 512], F32))
        s_c1 = _st.enter_context(nc.sbuf_tensor("b_c1", [128, 512], F32))
        s_c2 = _st.enter_context(nc.sbuf_tensor("b_c2", [128, 512], F32))
        p_mm = _st.enter_context(nc.psum_tensor("pb_mm", [128, 512], F32))
        for blk in range(4):
            tc = slice(blk * 512, (blk + 1) * 512)
            for k in range(64):
                p.dma(s_yst[:, :], ynT[k * 128:(k + 1) * 128, tc])
                p.vcopy(s_ybf[:, k * 512:(k + 1) * 512], s_yst[:, :])
            for ns in range(8):
                nsl = slice(ns * 512, (ns + 1) * 512)
                for k in range(64):
                    p.dma(s_wst[:, :], opT[k * 128:(k + 1) * 128, nsl])
                    p.vcopy(s_wbf[:, k * 512:(k + 1) * 512], s_wst[:, :])
                for m in range(4):
                    for k in range(64):
                        p.mm(p_mm[:, :],
                             s_ybf[:, k * 512 + m * 128:k * 512 + (m + 1) * 128],
                             s_wbf[:, k * 512:(k + 1) * 512],
                             first=(k == 0), last=(k == 63))
                    rr = slice((blk * 4 + m) * 128, (blk * 4 + m + 1) * 128)
                    p.dma(s_xn[:, :], xn[rr, nsl])
                    p.dma(s_x[:, :], x0[rr, nsl])
                    p.dma(s_f[:, :], f0[rr, nsl])
                    p.tt(s_c1[:, :], p_mm[:, :], s_xn[:, :], OP.add)
                    p.tt(s_c2[:, :], s_c1[:, :], s_x[:, :], OP.add)
                    p.tt(s_c1[:, :], s_c2[:, :], s_f[:, :], OP.add)
                    p.dma(x1_o[rr, nsl], s_c1[:, :])
        p.emit(nc)
    return nc



def build_l4c(nc):
    x1 = nc.dram_tensor("x1", [TPC, D], F32, kind="ExternalInput").ap()
    xpT = nc.dram_tensor("xpT", [D, 160], F32, kind="ExternalInput").ap()
    dtpT = nc.dram_tensor("dtpT", [128, D], F32, kind="ExternalInput").ap()
    dtpb = nc.dram_tensor("dtpb", [128, D], F32, kind="ExternalInput").ap()
    wbT = nc.dram_tensor("wbT", [D, D], F32, kind="ExternalInput").ap()
    bb = nc.dram_tensor("bb", [128, D], F32, kind="ExternalInput").ap()
    aT = nc.dram_tensor("aT", [D, 64], F32, kind="ExternalInput").ap()
    wlT = nc.dram_tensor("wlT", [D, 64], F32, kind="ExternalInput").ap()
    btT = nc.dram_tensor("btT", [64, D], F32, kind="ExternalInput").ap()
    wrT = nc.dram_tensor("wrT", [64, D], F32, kind="ExternalInput").ap()
    eye = nc.dram_tensor("eye", [128, 128], F32, kind="ExternalInput").ap()
    xn1_o = nc.dram_tensor("xn1_o", [TPC, D], F32, kind="ExternalOutput").ap()
    dt1_o = nc.dram_tensor("dt1_o", [TPC, D], F32, kind="ExternalOutput").ap()
    bc1_o = nc.dram_tensor("bc1_o", [TPC, 32], F32, kind="ExternalOutput").ap()
    ffn1_o = nc.dram_tensor("ffn1_o", [TPC, D], F32, kind="ExternalOutput").ap()
    p = Prog()
    with contextlib.ExitStack() as _st:
        s_t1 = _st.enter_context(nc.sbuf_tensor("c_t1", [128, D], F32))
        s_t2 = _st.enter_context(nc.sbuf_tensor("c_t2", [128, D], F32))
        s_t3 = _st.enter_context(nc.sbuf_tensor("c_t3", [128, D], F32))
        s_m = _st.enter_context(nc.sbuf_tensor("c_m", [128, 8], F32))
        s_xnbf = _st.enter_context(nc.sbuf_tensor("c_xnbf", [128, D], BF16))
        s_xnT = _st.enter_context(nc.sbuf_tensor("c_xnT", [128, 32 * 1024], BF16))
        s_eyef = _st.enter_context(nc.sbuf_tensor("c_eyef", [128, 128], F32))
        s_eyebf = _st.enter_context(nc.sbuf_tensor("c_eyebf", [128, 128], BF16))
        s_xpbf = _st.enter_context(nc.sbuf_tensor("c_xpbf", [128, 32 * 160], BF16))
        s_dtpbf = _st.enter_context(nc.sbuf_tensor("c_dtpbf", [128, D], BF16))
        s_bias = _st.enter_context(nc.sbuf_tensor("c_bias", [128, 512], F32))
        s_aTbf = _st.enter_context(nc.sbuf_tensor("c_abf", [128, 32 * 64], BF16))
        s_wlbf = _st.enter_context(nc.sbuf_tensor("c_wlbf", [128, 32 * 64], BF16))
        s_btbf = _st.enter_context(nc.sbuf_tensor("c_btbf", [64, D], BF16))
        s_wrbf = _st.enter_context(nc.sbuf_tensor("c_wrbf", [64, D], BF16))
        s_wst = _st.enter_context(nc.sbuf_tensor("c_wst", [128, 512], F32))
        s_wbf = _st.enter_context(nc.sbuf_tensor("c_wbf", [128, 32 * 512], BF16))
        s_xp = _st.enter_context(nc.sbuf_tensor("c_xp", [128, 160], F32))
        s_dtin = _st.enter_context(nc.sbuf_tensor("c_dtin", [128, 128], BF16))
        s_dtinT = _st.enter_context(nc.sbuf_tensor("c_dtinT", [128, 128], BF16))
        s_ab = _st.enter_context(nc.sbuf_tensor("c_ab", [128, 64], BF16))
        s_aT = _st.enter_context(nc.sbuf_tensor("c_aT", [64, 8 * 128], BF16))
        p_mm = _st.enter_context(nc.psum_tensor("pc_mm", [128, 512], F32))
        p_x = _st.enter_context(nc.psum_tensor("pc_x", [128, 160], F32))
        p_t = _st.enter_context(nc.psum_tensor("pc_t", [128, 128], BF16))
        p_a = _st.enter_context(nc.psum_tensor("pc_a", [128, 64], F32))
        p_aT = _st.enter_context(nc.psum_tensor("pc_aT", [64, 128], BF16))
        p.memset(s_m[:, 4:5], EPS)
        p.dma(s_eyef[:, :], eye[:, :])
        p.vcopy(s_eyebf[:, :], s_eyef[:, :])
        for k in range(32):
            p.dma(s_wst[:, 0:160], xpT[k * 128:(k + 1) * 128, :])
            p.vcopy(s_xpbf[:, k * 160:(k + 1) * 160], s_wst[:, 0:160])
            p.dma(s_wst[:, 0:64], aT[k * 128:(k + 1) * 128, :])
            p.vcopy(s_aTbf[:, k * 64:(k + 1) * 64], s_wst[:, 0:64])
            p.dma(s_wst[:, 0:64], wlT[k * 128:(k + 1) * 128, :])
            p.vcopy(s_wlbf[:, k * 64:(k + 1) * 64], s_wst[:, 0:64])
        p.dma(s_t1[:, :], dtpT[:, :])
        p.vcopy(s_dtpbf[:, :], s_t1[:, :])
        p.dma(s_t1[0:64, :], btT[:, :])
        p.vcopy(s_btbf[:, :], s_t1[0:64, :])
        p.dma(s_t1[0:64, :], wrT[:, :])
        p.vcopy(s_wrbf[:, :], s_t1[0:64, :])
        for tb in range(2):
            for j in range(8):
                t = tb * 8 + j
                p.dma(s_t1[:, :], x1[t * 128:(t + 1) * 128, :])
                p.red(s_m[:, 0:1], s_t1[:, :], OP.add)
                p.act(s_m[:, 1:2], s_m[:, 0:1], AF.Copy, scale=1.0 / D)
                p.ts(s_t2[:, :], s_t1[:, :], s_m[:, 1:2], None, OP.subtract)
                p.act(s_t3[:, :], s_t2[:, :], AF.Square)
                p.red(s_m[:, 2:3], s_t3[:, :], OP.add)
                p.act(s_m[:, 3:4], s_m[:, 2:3], AF.Ln, scale=1.0 / D,
                      bias=s_m[:, 4:5])
                p.act(s_m[:, 0:1], s_m[:, 3:4], AF.Exp, scale=-0.5)
                p.ts(s_t3[:, :], s_t2[:, :], s_m[:, 0:1], None, OP.mult)
                p.dma(xn1_o[t * 128:(t + 1) * 128, :], s_t3[:, :])
                p.vcopy(s_xnbf[:, :], s_t3[:, :])
                for k in range(32):
                    p.trans(p_t[:, :], s_xnbf[:, k * 128:(k + 1) * 128], s_eyebf[:, :])
                    p.copy(s_xnT[:, k * 1024 + j * 128:k * 1024 + (j + 1) * 128],
                           p_t[:, :])
            for j in range(8):
                t = tb * 8 + j
                rr = slice(t * 128, (t + 1) * 128)
                for k in range(32):
                    p.mm(p_x[:, :],
                         s_xnT[:, k * 1024 + j * 128:k * 1024 + (j + 1) * 128],
                         s_xpbf[:, k * 160:(k + 1) * 160],
                         first=(k == 0), last=(k == 31))
                p.copy(s_xp[:, :], p_x[:, :])
                p.dma(bc1_o[rr, :], s_xp[:, 128:160])
                p.vcopy(s_dtin[:, :], s_xp[:, 0:128])
                p.trans(p_t[:, :], s_dtin[:, :], s_eyebf[:, :])
                p.copy(s_dtinT[:, :], p_t[:, :])
                for ns in range(8):
                    nsl = slice(ns * 512, (ns + 1) * 512)
                    p.dma(s_bias[:, :], dtpb[:, nsl])
                    p.mm(p_mm[:, :], s_dtinT[:, :], s_dtpbf[:, nsl])
                    p.tt(s_t1[:, 0:512], p_mm[:, :], s_bias[:, :], OP.add)
                    p.act(s_t2[:, 0:512], s_t1[:, 0:512], AF.Abs)
                    p.act(s_t3[:, 0:512], s_t2[:, 0:512], AF.Exp, scale=-1.0)
                    p.act(s_t2[:, 0:512], s_t3[:, 0:512], AF.Ln, bias=1.0)
                    p.add("vector", lambda e, o=s_t3, i_=s_t1:
                          e.tensor_relu(o[:, 0:512], i_[:, 0:512]))
                    p.tt(s_t1[:, 0:512], s_t2[:, 0:512], s_t3[:, 0:512], OP.add)
                    p.dma(dt1_o[rr, nsl], s_t1[:, 0:512])
                for k in range(32):
                    p.mm(p_a[:, :],
                         s_xnT[:, k * 1024 + j * 128:k * 1024 + (j + 1) * 128],
                         s_aTbf[:, k * 64:(k + 1) * 64],
                         first=(k == 0), last=False)
                for k in range(32):
                    p.mm(p_a[:, :],
                         s_xnT[:, k * 1024 + j * 128:k * 1024 + (j + 1) * 128],
                         s_wlbf[:, k * 64:(k + 1) * 64],
                         first=False, last=(k == 31))
                p.act(s_ab[:, :], p_a[:, :], AF.Copy, scale=0.25)
                p.trans(p_aT[:, :], s_ab[:, :], s_eyebf[:, :])
                p.copy(s_aT[:, j * 128:(j + 1) * 128], p_aT[:, :])
            for ns in range(8):
                nsl = slice(ns * 512, (ns + 1) * 512)
                p.dma(s_bias[:, :], bb[:, nsl])
                for k in range(32):
                    p.dma(s_wst[:, :], wbT[k * 128:(k + 1) * 128, nsl])
                    p.vcopy(s_wbf[:, k * 512:(k + 1) * 512], s_wst[:, :])
                for j in range(8):
                    t = tb * 8 + j
                    for k in range(32):
                        p.mm(p_mm[:, :],
                             s_xnT[:, k * 1024 + j * 128:k * 1024 + (j + 1) * 128],
                             s_wbf[:, k * 512:(k + 1) * 512],
                             first=(k == 0), last=False)
                    p.mm(p_mm[:, :], s_aT[:, j * 128:(j + 1) * 128],
                         s_btbf[:, nsl], first=False, last=False)
                    p.mm(p_mm[:, :], s_aT[:, j * 128:(j + 1) * 128],
                         s_wrbf[:, nsl], first=False, last=True)
                    p.tt(s_t1[:, 0:512], p_mm[:, :], s_bias[:, :], OP.add)
                    p.dma(ffn1_o[t * 128:(t + 1) * 128, nsl], s_t1[:, 0:512])
        p.emit(nc)
    return nc



def build_l5(nc):
    xc = nc.dram_tensor("xc", [2048, L], F32, kind="ExternalInput").ap()
    dtc = nc.dram_tensor("dtc", [2048, L], F32, kind="ExternalInput").ap()
    bbc = nc.dram_tensor("bbc", [16 * 128, L], F32, kind="ExternalInput").ap()
    cbc = nc.dram_tensor("cbc", [16 * 128, L], F32, kind="ExternalInput").ap()
    a1 = nc.dram_tensor("a1", [128, 256], F32, kind="ExternalInput").ap()
    d1 = nc.dram_tensor("d1", [128, 16], F32, kind="ExternalInput").ap()
    x1c = nc.dram_tensor("x1c", [2048, L], F32, kind="ExternalInput").ap()
    f1c = nc.dram_tensor("f1c", [2048, L], F32, kind="ExternalInput").ap()
    o_cm = nc.dram_tensor("o_cm", [2048, L], F32, kind="ExternalOutput").ap()
    p = Prog()
    with contextlib.ExitStack() as _st:
        s_A = _st.enter_context(nc.sbuf_tensor("e_A", [128, 256], F32))
        s_Ae = _st.enter_context(nc.sbuf_tensor("e_Ae", [128, 256], F32))
        s_D = _st.enter_context(nc.sbuf_tensor("e_D", [128, 16], F32))
        s_xc = _st.enter_context(nc.sbuf_tensor("e_xc", [128, L], F32))
        s_dt = _st.enter_context(nc.sbuf_tensor("e_dt", [128, L], F32))
        s_xdt = _st.enter_context(nc.sbuf_tensor("e_xdt", [128, L], F32))
        s_S = _st.enter_context(nc.sbuf_tensor("e_S", [128, L], F32))
        s_dA = _st.enter_context(nc.sbuf_tensor("e_dA", [128, L], F32))
        s_B = _st.enter_context(nc.sbuf_tensor("e_B", [128, L], F32))
        s_u = _st.enter_context(nc.sbuf_tensor("e_u", [128, L], F32))
        s_h = _st.enter_context(nc.sbuf_tensor("e_h", [128, L], F32))
        s_ya = _st.enter_context(nc.sbuf_tensor("e_ya", [128, L], F32))
        p.dma(s_Ae[:, :], a1[:, :])
        p.act(s_A[:, :], s_Ae[:, :], AF.Exp)
        p.ts(s_A[:, :], s_A[:, :], -1.0, None, OP.mult)
        p.dma(s_D[:, :], d1[:, :])
        for dt in range(16):
            rr = slice(dt * 128, (dt + 1) * 128)
            p.dma(s_xc[:, :], xc[rr, :])
            p.dma(s_dt[:, :], dtc[rr, :])
            p.tt(s_xdt[:, :], s_xc[:, :], s_dt[:, :], OP.mult)
            p.memset(s_ya[:, :], 0.0)
            for n in range(16):
                p.ts(s_S[:, :], s_dt[:, :], s_A[:, dt * 16 + n:dt * 16 + n + 1],
                     None, OP.mult)
                p.act(s_dA[:, :], s_S[:, :], AF.Exp)
                p.dma(s_B[:, :], bbc[n * 128:(n + 1) * 128, :])
                p.tt(s_u[:, :], s_xdt[:, :], s_B[:, :], OP.mult)
                p.add("vector", lambda e, o=s_h, a=s_dA, b=s_u:
                      e.tensor_tensor_scan(o[:, :], a[:, :], b[:, :], 0.0,
                                           OP.mult, OP.add))
                p.dma(s_B[:, :], cbc[n * 128:(n + 1) * 128, :])
                p.tt(s_u[:, :], s_h[:, :], s_B[:, :], OP.mult)
                p.tt(s_ya[:, :], s_ya[:, :], s_u[:, :], OP.add)
            p.stt(s_S[:, :], s_xc[:, :], s_D[:, dt:dt + 1], s_ya[:, :],
                  OP.mult, OP.add)
            p.dma(s_xc[:, :], x1c[rr, :])
            p.dma(s_dt[:, :], f1c[rr, :])
            p.tt(s_xdt[:, :], s_S[:, :], s_xc[:, :], OP.add)
            p.tt(s_S[:, :], s_xdt[:, :], s_dt[:, :], OP.add)
            p.dma(o_cm[rr, :], s_S[:, :])
        p.emit(nc)
    return nc



_DBG = {}
_DBG_ON = False


def kernel(**inp):
    x_full = ct(inp["L0_attn_hidden_states"]).reshape(TOK, D)
    eye = np.eye(128, dtype=np.float32)

    winT = ct(inp["L0_attn_in_proj_w"].T)
    wupT = ct(inp["L0_ffn_W_up"].T)
    gateT = ct(inp["L0_ffn_gate_w"].T)
    ebias = ct(np.broadcast_to(np.asarray(inp["L0_ffn_expert_bias"]), (128, 4)))
    laT = ct(np.asarray(inp["L0_ffn_lora_A"]).transpose(2, 0, 1).reshape(D, 32))
    lb = np.asarray(inp["L0_ffn_lora_B"])
    lbT = ct(np.concatenate([lb[e].T for e in range(4)], axis=1))
    maps = []
    for c in range(NCORES):
        maps.append({
            "x": x_full[c * TPC:(c + 1) * TPC],
            "winT": winT, "wupT": wupT, "gateT": gateT, "ebias": ebias,
            "laT": laT, "lbT": lbT, "eye": eye,
        })
    r1 = runl(build_l1, maps)
    xn_full = np.concatenate([r["xn_o"] for r in r1], axis=0)
    zT = np.concatenate([r["zT_o"] for r in r1], axis=1)
    ffn0 = np.concatenate([r["ffn_o"] for r in r1], axis=0)
    if _DBG_ON:
        _DBG.update(xn=xn_full, zT=zT, ffn0=ffn0)

    xBC_T = zT[DI:DI + CONVD]
    rows = np.concatenate(
        [np.pad(xBC_T[:, b * L:(b + 1) * L], ((0, 0), (3, 0))) for b in range(B)],
        axis=0)
    cw_all = np.tile(ct(inp["L0_attn_conv_w"]), (B, 1))
    cb_all = np.tile(ct(inp["L0_attn_conv_b"]).reshape(-1, 1), (B, 1))
    maps = []
    for c in range(NCORES):
        maps.append({"xp": ct(rows[c * 5120:(c + 1) * 5120]),
                     "cw": ct(cw_all[c * 5120:(c + 1) * 5120]),
                     "cb": ct(cb_all[c * 5120:(c + 1) * 5120])})
    r2 = runl(build_l2, maps)
    xact = np.concatenate([r["xo"] for r in r2], axis=0).reshape(B, CONVD, L)
    if _DBG_ON:
        _DBG.update(xact=xact)

    A_log = ct(inp["L0_attn_A_log"])
    dt_bias = ct(inp["L0_attn_dt_bias"])
    D0 = ct(inp["L0_attn_D"])
    tri = np.triu(np.ones((128, 128), dtype=np.float32))
    mka = np.where(tri > 0, 0.0, -1e5).astype(np.float32)
    on1 = np.ones((1, 128), dtype=np.float32)
    dtrawT = zT[DI + CONVD:]
    maps = []
    for c in range(NCORES):
        b, hh = c // 2, c % 2
        maps.append({
            "x_tm": ct(xact[b, hh * 4096:(hh + 1) * 4096, :].T),
            "b_cm": ct(xact[b, DI + hh * 512:DI + (hh + 1) * 512, :]),
            "c_cm": ct(xact[b, DI + 1024 + hh * 512:DI + 1024 + (hh + 1) * 512, :]),
            "bt": ct(xact[b, DI + hh * 512:DI + (hh + 1) * 512, :].T),
            "dtr": ct(dtrawT[hh * 64:(hh + 1) * 64, b * L:(b + 1) * L].T),
            "arep": ct(np.broadcast_to(A_log[hh * 64:(hh + 1) * 64], (128, 64))),
            "dtb": ct(np.broadcast_to(dt_bias[hh * 64:(hh + 1) * 64], (128, 64))),
            "drep": ct(np.broadcast_to(D0[hh * 64:(hh + 1) * 64], (128, 64))),
            "tri": tri, "mka": mka, "on1": on1,
        })
    r3 = runl(build_l3, maps)

    Y_tok = np.empty((TOK, DI), dtype=np.float32)
    for c in range(NCORES):
        b, hh = c // 2, c % 2
        Y_tok[b * L:(b + 1) * L, hh * 4096:(hh + 1) * 4096] = r3[c]["y_o"]
    if _DBG_ON:
        _DBG.update(Y=Y_tok.copy())
    z_tok = ct(zT[0:DI].T)
    nw_rep = ct(np.broadcast_to(ct(inp["L0_attn_norm_w"]), (128, DI)))
    maps = [{"Y": Y_tok[c * TPC:(c + 1) * TPC],
             "z": z_tok[c * TPC:(c + 1) * TPC], "nw": nw_rep}
            for c in range(NCORES)]
    r4a = runl(build_l4a, maps)
    yn_full = np.concatenate([r["yn_o"] for r in r4a], axis=0)
    if _DBG_ON:
        _DBG.update(yn=yn_full)

    ynT = ct(yn_full.T)
    opT = ct(inp["L0_attn_out_proj_w"].T)
    maps = [{"ynT": ct(ynT[:, c * TPC:(c + 1) * TPC]), "opT": opT,
             "xn": xn_full[c * TPC:(c + 1) * TPC],
             "x0": x_full[c * TPC:(c + 1) * TPC],
             "f0": ffn0[c * TPC:(c + 1) * TPC]}
            for c in range(NCORES)]
    r4b = runl(build_l4b, maps)
    x1_full = np.concatenate([r["x1_o"] for r in r4b], axis=0)
    if _DBG_ON:
        _DBG.update(x1=x1_full)

    xpT = ct(inp["L1_attn_x_proj_w"].T)
    dtpT = ct(inp["L1_attn_dt_proj_w"].T)
    dtpb = ct(np.broadcast_to(ct(inp["L1_attn_dt_proj_b"]), (128, D)))
    wbT = ct(inp["L1_ffn_W_base"].T)
    bb = ct(np.broadcast_to(ct(inp["L1_ffn_b_base"]), (128, D)))
    aT = ct(inp["L1_ffn_A"].T)
    wlT = ct(inp["L1_ffn_WL"].T)
    btT = ct(inp["L1_ffn_B"].T)
    wrT = ct(inp["L1_ffn_WR"].T)
    maps = [{"x1": x1_full[c * TPC:(c + 1) * TPC], "xpT": xpT, "dtpT": dtpT,
             "dtpb": dtpb, "wbT": wbT, "bb": bb, "aT": aT, "wlT": wlT,
             "btT": btT, "wrT": wrT, "eye": eye}
            for c in range(NCORES)]
    r4c = runl(build_l4c, maps)
    xn1_full = np.concatenate([r["xn1_o"] for r in r4c], axis=0)
    dt1_full = np.concatenate([r["dt1_o"] for r in r4c], axis=0)
    bc1 = np.concatenate([r["bc1_o"] for r in r4c], axis=0)
    ffn1_full = np.concatenate([r["ffn1_o"] for r in r4c], axis=0)
    if _DBG_ON:
        _DBG.update(xn1=xn1_full, dt1=dt1_full, bc1=bc1, ffn1=ffn1_full)

    A_log1 = ct(inp["L1_attn_A_log"])
    D1 = ct(inp["L1_attn_D"])
    maps = []
    for c in range(NCORES):
        b, dh = c // 2, c % 2
        rsl = slice(dh * 2048, (dh + 1) * 2048)
        tsl = slice(b * L, (b + 1) * L)
        B1b = bc1[tsl, 0:16]
        C1b = bc1[tsl, 16:32]
        bbc = ct(np.broadcast_to(B1b.T[:, None, :], (16, 128, L)).reshape(2048, L))
        cbc = ct(np.broadcast_to(C1b.T[:, None, :], (16, 128, L)).reshape(2048, L))
        a1 = ct(A_log1[rsl].reshape(16, 128, 16).transpose(1, 0, 2).reshape(128, 256))
        d1 = ct(D1[rsl].reshape(16, 128).T)
        maps.append({
            "xc": ct(xn1_full[tsl, rsl].T),
            "dtc": ct(dt1_full[tsl, rsl].T),
            "bbc": bbc, "cbc": cbc, "a1": a1, "d1": d1,
            "x1c": ct(x1_full[tsl, rsl].T),
            "f1c": ct(ffn1_full[tsl, rsl].T),
        })
    r5 = runl(build_l5, maps)
    out = np.empty((TOK, D), dtype=np.float32)
    for c in range(NCORES):
        b, dh = c // 2, c % 2
        out[b * L:(b + 1) * L, dh * 2048:(dh + 1) * 2048] = r5[c]["o_cm"].T
    return out.reshape(B, L, D)
